# version 04 (final)
# baseline (speedup 1.0000x reference)
import os
os.environ.setdefault("TRN_ALLOW_ICI_ALL_TO_ALL", "1")

import math
from contextlib import ExitStack

import numpy as np
import ml_dtypes

import concourse.bass as bass
import concourse.mybir as mybir
from concourse.bass_utils import run_bass_kernel_spmd
from concourse.aot_env import allow_ici_all_to_all

F32 = mybir.dt.float32
BF16 = mybir.dt.bfloat16
I32 = mybir.dt.int32
AF = mybir.ActivationFunctionType
ALU = mybir.AluOpType
AXX = mybir.AxisListType.X

B, S, D, H, HD = 2, 8192, 2048, 16, 128
INTER = 8192
NHL = 4
OWN = 2048
DC = D // 128
IC = INTER // 128
NBT = S // 512
NOT_ = OWN // 512
NQT = S // 512
SCALE = 1.0 / math.sqrt(HD)
RS2048 = 1.0 / math.sqrt(2048.0)
RS8192 = 1.0 / math.sqrt(8192.0)


def _hadamard(n):
    i = np.arange(n)
    ij = i[:, None] & i[None, :]
    pop = np.zeros((n, n), dtype=np.int64)
    v = ij.copy()
    while v.any():
        pop += v & 1
        v >>= 1
    return np.where(pop % 2 == 0, 1.0, -1.0).astype(np.float32)


def _head_params(slopes):
    G, WT, WQ = [], [], []
    for j in range(NHL):
        grp = [float(slopes[4 * j + q]) for q in range(4)]
        s_steep = max(max(grp), 1e-9)
        s_mild = max(min(grp), 1e-9)
        lnfac = math.log(2.0 / min(1.0, s_steep))
        g = int((67.0 - lnfac) // (64.0 * s_steep))
        g = min(g, int(62.0 // (64.0 * s_steep)))
        g = max(1, min(64, g))
        wq = 128
        for cand in (512, 256):
            if s_steep * max(0, cand - 64 * g) <= 58.0:
                wq = cand
                break
        G.append(g)
        WQ.append(wq)
        wt = int(math.ceil((wq + 160.0 / s_mild) / 128.0))
        WT.append(max(wq // 128, min(64, wt)))
    return G, WT, WQ


class E:

    def __init__(self):
        self.emit = False
        self.cnt = {}
        self.marks = {}
        self.handles = {}

    def reset_counts(self):
        for k in self.cnt:
            self.cnt[k] = 0

    def sem(self, name, handle):
        self.cnt[name] = 0
        self.handles[name] = handle

    def inc(self, mk, name, k=1, label=None):
        if self.emit:
            ins = mk()
            if ins is not None:
                ins.then_inc(self.handles[name], k)
        self.cnt[name] += k
        if label is not None and not self.emit:
            self.marks[label] = self.cnt[name]
        return self.cnt[name]

    def do(self, mk):
        if self.emit:
            mk()

    def mark(self, name, label):
        if not self.emit:
            self.marks[label] = self.cnt[name]

    def wait(self, eng, name, label=None, value=None, delta=0):
        if self.emit:
            v = (self.marks[label] if label is not None else value) + delta
            if v > 0:
                eng.wait_ge(self.handles[name], v)


def _schedule_units(GJ, WTJ, WQJ):
    units = []
    partials = set()
    for j in range(NHL):
        wq = WQJ[j]
        g_ = GJ[j]
        wt = ((WTJ[j] + g_ - 1) // g_) * g_
        for t in range(S // wq):
            qt0 = wq * t
            nkt_full = (qt0 + wq) // 128
            nkt = min(nkt_full, wt)
            t_start = nkt_full - nkt
            sz0 = nkt % g_
            tiles = []
            for i in range(nkt):
                if i < sz0:
                    g, jg, psz = 0, i, sz0
                    first, last = (i == 0), (i == sz0 - 1)
                else:
                    g = (i - sz0) // g_ + (1 if sz0 else 0)
                    jg = (i - sz0) % g_
                    psz = 0
                    first, last = (jg == 0), (jg == g_ - 1 or i == nkt - 1)
                T = t_start + i
                tiles.append(dict(
                    i=i, g=g, jg=jg, T=T, psz=psz,
                    diag=(128 * T + 127 > qt0),
                    first=first, last=last))
            if sz0:
                partials.add((j, sz0))
            ng = (1 if sz0 else 0) + (nkt - sz0) // g_
            units.append(dict(j=j, t=t, w=wq, qt0=qt0, t_start=t_start, nkt=nkt,
                              sz0=sz0, ng=ng, tiles=tiles))
    gi = 0
    mcount = 0
    for ui, u in enumerate(units):
        u["ui"] = ui
        for tl in u["tiles"]:
            tl["gi"] = gi
            gi += 1
            if tl["diag"]:
                mcount += 1
                tl["m"] = mcount
            tl["_u"] = u
    all_tiles = [tl for u in units for tl in u["tiles"]]
    diag_tiles = [tl for tl in all_tiles if tl["diag"]]
    return units, all_tiles, diag_tiles, sorted(partials)


class _Null:
    def __enter__(self):
        return None

    def __exit__(self, *a):
        return False


def build_program(GJ, WTJ, WQJ, part=0):
    nc = bass.Bass("TRN2", target_bir_lowering=False)
    units, all_tiles, diag_tiles, partials = _schedule_units(GJ, WTJ, WQJ)
    bias_off = [sum(GJ[:j]) for j in range(NHL)]
    pcol = {}
    off = sum(GJ)
    for (j, sz) in partials:
        pcol[(j, sz)] = off
        off += sz
    NBIAS = off
    prcol = {}
    off = NHL
    for (j, sz) in partials:
        prcol[(j, sz)] = off
        off += 1
    NRTAB = off

    def din(name, shape, dt=F32):
        return nc.dram_tensor(name, shape, dt, kind="ExternalInput").ap()

    x_batch = din("x_batch", [S, D])
    x_own = din("x_own", [OWN, D])
    wq_my = din("wq_my", [NHL * HD, D])
    wk_my = din("wk_my", [NHL * HD, D])
    wv_my = din("wv_my", [NHL * HD, D])
    bq_my = din("bq_my", [NHL * HD])
    bk_my = din("bk_my", [NHL * HD])
    bv_my = din("bv_my", [NHL * HD])
    slopes_my = din("slopes_my", [NHL])
    out_w = din("out_w", [D, D])
    out_b = din("out_b", [D])
    w_up = din("w_up", [INTER, D])
    w_gate = din("w_gate", [INTER, D])
    w_down = din("w_down", [D, INTER])
    su_ug_i = din("su_ug", [D])
    su_dp_i = din("su_dp", [INTER])
    ws_ug_i = din("ws_ug", [2 * INTER])
    ws_dp_i = din("ws_dp", [D])
    h128_i = din("h128", [128, 128])
    iden_i = din("iden", [128, 128])
    bm0_i = din("bm0", [128, 1])
    bm1_i = din("bm1", [128, 1])
    y_own = nc.dram_tensor("y_own", [OWN, D], F32, kind="ExternalOutput").ap()

    def dram(name, shape, dt=BF16):
        return nc.dram_tensor(name, shape, dt)

    wq_bf = dram("wq_bf", [NHL * HD, D])
    wk_bf = dram("wk_bf", [NHL * HD, D])
    wv_bf = dram("wv_bf", [NHL * HD, D])
    outw_bf = dram("outw_bf", [D, D])
    wup_bf = dram("wup_bf", [INTER, D])
    wgate_bf = dram("wgate_bf", [INTER, D])
    wdown_bf = dram("wdown_bf", [D, INTER])
    wupT = dram("wupT", [D, INTER])
    wgateT = dram("wgateT", [D, INTER])
    wdownT = dram("wdownT", [INTER, D])
    qbuf = dram("qbuf", [NHL, HD, S])
    kbuf = dram("kbuf", [NHL, HD, S])
    vbuf = dram("vbuf", [NHL, S // 128, 128, HD])
    if part == 1:
        av_local = nc.dram_tensor("av_local", [OWN, OWN], BF16, kind="ExternalOutput")
    else:
        av_local = dram("av_local", [OWN, OWN])
    av_in = (nc.dram_tensor("av_in", [OWN, OWN], BF16, kind="ExternalInput")
             if part == 2 else None)
    a2a_in = dram("a2a_in", [2 * OWN, OWN])
    a2a_out = dram("a2a_out", [2 * OWN, OWN])
    av_my = dram("av_my", [OWN, OWN])
    ffnfm = dram("ffnfm", [D, OWN])

    es = ExitStack()
    em = E()

    def sb(name, shape, dtype, st=None):
        if st is not None and not em.emit:
            name = name + "_rec"
        return (st or es).enter_context(nc.sbuf_tensor(name, shape, dtype))

    def psum(name, shape, dtype=F32, st=None):
        if st is not None and not em.emit:
            name = name + "_rec"
        return (st or es).enter_context(nc.psum_tensor(name, shape, dtype))

    h128_bf = sb("h128_bf", [128, 128], BF16)
    h128n = sb("h128n", [128, 128], BF16)
    iden_bf = sb("iden_bf", [128, 128], BF16)
    suh_ug = sb("suh_ug", [128, DC * 128], BF16)
    su_ug_pc = sb("su_ug_pc", [128, DC], F32)
    su_dp_pc = sb("su_dp_pc", [128, IC], F32)
    wsu_eff = sb("wsu_eff", [128, IC], F32)
    wsg_pc = sb("wsg_pc", [128, IC], F32)
    wsdp_pc = sb("wsdp_pc", [128, DC], F32)
    bq_pc = sb("bq_pc", [128, NHL], F32)
    bk_pc = sb("bk_pc", [128, NHL], F32)
    bvb = sb("bvb", [128, NHL * HD], BF16)
    outbb = sb("outbb", [128, D], BF16)
    slope_bc = sb("slope_bc", [128, NHL], F32)
    iota_col = sb("iota_col", [128, 1], F32)
    ones_bf = sb("ones_bf", [128, 1], BF16)
    ones_row_f = sb("ones_row_f", [1, 128], F32)
    eps_col = sb("eps_col", [128, 1], F32)
    biastab = sb("biastab", [128, max(NBIAS, 1)], F32)
    rtab = sb("rtab", [128, NRTAB], F32)
    bm0_sb = sb("bm0_sb", [128, 1], F32)
    bm1_sb = sb("bm1_sb", [128, 1], F32)

    SEMS = ["sCast", "sSmall", "sCP", "sCV", "sCA",
            "sX0", "sX1", "sX2", "sX3", "sLN", "sRS", "sXN", "sTR", "sTE", "sPJ", "sEV",
            "sST0", "sST1", "sWD", "sWP0", "sWP1", "sPR0", "sPR1",
            "sKV0", "sKV1", "sS_", "sP_", "sM_", "sAV", "sR_", "sRL", "sBC",
            "sAB", "sAO0", "sAO1", "sCC", "sCC2", "sCC3",
            "s3X0", "s3X1", "s3X2", "s3X3", "s3LN", "s3RS", "s3XN", "s3TR", "s3TE",
            "s3W0", "s3W1", "s3W2", "s3W3", "s3WD0", "s3WD1", "s3PE", "s3V",
            "s3SIL", "s3H", "s3FF", "s3ST0", "s3ST1",
            "s4W0", "s4W1", "s4H0", "s4H1", "s4WO", "s4X0", "s4X1", "s4PE", "s4EV", "s4F", "s4ST0", "s4ST1"]
    for s_ in SEMS:
        em.sem(s_, es.enter_context(nc.semaphore(s_)))

    bf = {}

    def p0_gp(g):
        casts = ((wq_bf, wq_my), (wk_bf, wk_my), (wv_bf, wv_my),
                 (outw_bf, out_w), (wup_bf, w_up),
                 (wgate_bf, w_gate), (wdown_bf, w_down))
        if part == 1:
            casts = casts[:3]
        elif part == 2:
            casts = casts[3:]
        for dst, src in casts:
            em.inc(lambda d=dst, s=src: g.dma_start(out=d.ap(), in_=s), "sCast", 16)
        em.mark("sCast", "casts_done")
        small = [
            (h128_bf[:, :], h128_i),
            (iden_bf[:, :], iden_i),
            (su_ug_pc[:, :], su_ug_i.rearrange("(c p) -> p c", p=128)),
            (su_dp_pc[:, :], su_dp_i.rearrange("(c p) -> p c", p=128)),
            (wsu_eff[:, :], ws_ug_i[0:INTER].rearrange("(c p) -> p c", p=128)),
            (wsg_pc[:, :], ws_ug_i[INTER:2 * INTER].rearrange("(c p) -> p c", p=128)),
            (wsdp_pc[:, :], ws_dp_i.rearrange("(c p) -> p c", p=128)),
            (bq_pc[:, :], bq_my.rearrange("(c p) -> p c", p=128)),
            (bk_pc[:, :], bk_my.rearrange("(c p) -> p c", p=128)),
            (bf["bv_row"][:, :], bv_my[None, :]),
            (bf["outb_row"][:, :], out_b[None, :]),
            (bf["slope_row"][:, :], slopes_my[None, :]),
            (bm0_sb[:, :], bm0_i),
            (bm1_sb[:, :], bm1_i),
        ]
        for dst, src in small:
            em.inc(lambda d=dst, s=src: g.dma_start(
                out=d, in_=s, allow_slow_non_contiguous=True), "sSmall", 16)
        em.do(lambda: g.memset(ones_bf[:, :], 1.0))
        em.do(lambda: g.memset(ones_row_f[:, :], 1.0))
        em.do(lambda: g.memset(eps_col[:, :], 1e-5))
        em.do(lambda: g.iota(iota_col[:, :], [[0, 1]], base=0, channel_multiplier=1,
                             allow_small_or_imprecise_dtypes=True))
        em.mark("sSmall", "small_done")
        em.do(lambda: g.wait_ge(em.handles["sSmall"], em.marks["small_done"]))
        em.do(lambda: g.wait_ge(em.handles["sCast"], em.marks["casts_done"]))

    def p0c_pe(t):
        ps_cb = bf["ps_cb"]
        em.inc(lambda: t.matmul(ps_cb[:, 0:NHL * HD], ones_row_f[:, :], bf["bv_row"][:, :],
                                start=True, stop=True), "sCP", 1, "cp_bvb")
        for i in range(4):
            em.wait(t, "sCV", "cv_bvb" if i == 0 else f"cv_outb{i - 1}")
            em.inc(lambda i=i: t.matmul(ps_cb[:, :], ones_row_f[:, :],
                                        bf["outb_row"][:, 512 * i:512 * (i + 1)],
                                        start=True, stop=True), "sCP", 1, f"cp_outb{i}")
        em.wait(t, "sCV", "cv_outb3")
        em.inc(lambda: t.matmul(ps_cb[:, 0:NHL], ones_row_f[:, :], bf["slope_row"][:, :],
                                start=True, stop=True), "sCP", 1, "cp_slope")

    def p0c_dve(v):
        ps_cb = bf["ps_cb"]
        em.wait(v, "sCP", "cp_bvb")
        em.inc(lambda: v.tensor_copy(bvb[:, :], ps_cb[:, 0:NHL * HD]), "sCV", 1, "cv_bvb")
        for i in range(4):
            em.wait(v, "sCP", f"cp_outb{i}")
            em.inc(lambda i=i: v.tensor_copy(outbb[:, 512 * i:512 * (i + 1)], ps_cb[:, :]),
                   "sCV", 1, f"cv_outb{i}")
        em.wait(v, "sCP", "cp_slope")
        n = em.inc(lambda: v.tensor_copy(slope_bc[:, :], ps_cb[:, 0:NHL]), "sCV", 1)
        em.wait(v, "sCV", value=n)
        for c in range(DC):
            em.do(lambda c=c: v.tensor_scalar(suh_ug[:, 128 * c:128 * (c + 1)],
                                              h128_bf[:, :], su_ug_pc[:, c:c + 1],
                                              RS2048, ALU.mult, ALU.mult))
        em.do(lambda: v.tensor_scalar(h128n[:, :], h128_bf[:, :], RS8192, None, ALU.mult))
        em.do(lambda: v.tensor_tensor(wsu_eff[:, :], wsu_eff[:, :], su_dp_pc[:, :],
                                      ALU.mult))
        for j in range(NHL):
            for jg in range(GJ[j]):
                col = bias_off[j] + jg
                em.do(lambda col=col, jg=jg, j=j: v.scalar_tensor_tensor(
                    biastab[:, col:col + 1], iota_col[:, :],
                    float(128 * jg - 64 * GJ[j]), slope_bc[:, j:j + 1],
                    ALU.add, ALU.mult))
        for (j, sz) in partials:
            for jg in range(sz):
                col = pcol[(j, sz)] + jg
                em.do(lambda col=col, jg=jg, j=j, sz=sz: v.scalar_tensor_tensor(
                    biastab[:, col:col + 1], iota_col[:, :],
                    float(128 * jg - 64 * sz), slope_bc[:, j:j + 1],
                    ALU.add, ALU.mult))
        for j in range(NHL):
            em.inc(lambda j=j: v.tensor_scalar(rtab[:, j:j + 1], slope_bc[:, j:j + 1],
                                               float(-64.0 * GJ[j]), None, ALU.mult),
                   "sCV", 1)
        for (j, sz) in partials:
            col = prcol[(j, sz)]
            em.inc(lambda col=col, j=j, sz=sz: v.tensor_scalar(
                rtab[:, col:col + 1], slope_bc[:, j:j + 1],
                float(-32.0 * (sz + GJ[j])), None, ALU.mult), "sCV", 1)
        em.mark("sCV", "cv_tables")

    def p0c_act(a):
        em.wait(a, "sCV", "cv_tables")
        em.inc(lambda: a.activation(rtab[:, :], rtab[:, :], AF.Exp, bias=0.0, scale=1.0),
               "sCA", 1, "ca_rtab")

    NPREP = 2 * DC + IC
    NGS = 4 * NBT

    def p1_scalar_dma(a):
        em.wait(a, "sCast", "casts_done")
        for wsb, wbf in ((bf["wqT_sb"], wq_bf), (bf["wkT_sb"], wk_bf),
                         (bf["wvT_sb"], wv_bf)):
            for dcx in range(DC):
                em.inc(lambda wsb=wsb, wbf=wbf, dcx=dcx: a.dma_start(
                    out=wsb[:, 512 * dcx:512 * (dcx + 1)],
                    in_=wbf.ap()[:, 128 * dcx:128 * (dcx + 1)], transpose=True),
                    "sWD", 16)
        em.mark("sWD", "qkvT_done")

    def _prep_item(s, kk):
        wprep = bf["wprep"]
        slot = kk % 2
        if kk >= 2:
            em.wait(s, f"sPR{kk % 2}", value=16 * ((kk - 2) // 2 + 1))
        if kk < DC:
            colap = wup_bf.ap()[:, 128 * kk:128 * (kk + 1)]
            rows, width = wupT.ap()[128 * kk:128 * (kk + 1), :], INTER
        elif kk < 2 * DC:
            colap = wgate_bf.ap()[:, 128 * (kk - DC):128 * (kk - DC) + 128]
            rows, width = wgateT.ap()[128 * (kk - DC):128 * (kk - DC) + 128, :], INTER
        else:
            colap = wdown_bf.ap()[:, 128 * (kk - 2 * DC):128 * (kk - 2 * DC) + 128]
            rows, width = wdownT.ap()[128 * (kk - 2 * DC):128 * (kk - 2 * DC) + 128, :], D
        n = em.inc(lambda colap=colap, slot=slot, width=width, wprep=wprep: s.dma_start(
            out=wprep[:, INTER * slot:INTER * slot + width],
            in_=colap, transpose=True), f"sWP{kk % 2}", 16)
        return (n, rows, slot, width, kk)

    def _prep_store(s, item):
        wprep = bf["wprep"]
        n, rows, slot, width, kk = item
        em.wait(s, f"sWP{kk % 2}", value=n)
        em.inc(lambda rows=rows, slot=slot, width=width, wprep=wprep: s.dma_start(
            out=rows, in_=wprep[:, INTER * slot:INTER * slot + width]),
            f"sPR{kk % 2}", 16)

    def _p1_store_tile(s, bt):
        for pi in range(12):
            gidx = bt * 12 + pi
            em.wait(s, "sEV", value=gidx + 1)
            if pi < 4:
                dst = qbuf.ap()[pi, :, 512 * bt:512 * (bt + 1)]
                src = bf["ev_q"][:, 512 * (gidx % 2):512 * (gidx % 2 + 1)]
            elif pi < 8:
                dst = kbuf.ap()[pi - 4, :, 512 * bt:512 * (bt + 1)]
                src = bf["ev_k"][:, 512 * (gidx % 2):512 * (gidx % 2 + 1)]
            else:
                T = 4 * bt + (pi - 8)
                dst = vbuf.ap()[:, T, :, :].rearrange("l p h -> p l h")
                src = bf["ev_v"][:, 512 * (gidx % 2):512 * (gidx % 2 + 1)].rearrange(
                    "p (l h) -> p l h", l=NHL)
            em.inc(lambda dst=dst, src=src: s.dma_start(out=dst, in_=src),
                   f"sST{gidx % 2}", 16)

    def p1_sync(s):
        x_sub = bf["x_sub"]
        for bt in range(NBT):
            for s_ in range(4):
                gs = 4 * bt + s_
                slot = gs % 2
                if gs >= 4:
                    em.wait(s, "sXN", value=gs - 3)
                em.inc(lambda gs=gs, x_sub=x_sub: s.dma_start(
                    out=x_sub[:, D * (gs % 4):D * (gs % 4 + 1)],
                    in_=x_batch[128 * gs:128 * (gs + 1), :]), f"sX{gs % 4}", 16)
            if bt >= 1:
                _p1_store_tile(s, bt - 1)
        _p1_store_tile(s, NBT - 1)
        em.mark("sST0", "p1_stores0")
        em.mark("sST1", "p1_stores1")
        em.do(lambda: s.wait_ge(em.handles["sST0"], em.marks["p1_stores0"]))
        em.do(lambda: s.wait_ge(em.handles["sST1"], em.marks["p1_stores1"]))

    def _ln_mean(v, gs, x_sub, mean_c, sems):
        sX_, sLN_, sRS_, sXN_, sTR_, sCVn = sems
        sl4 = gs % 4
        xs = x_sub[:, D * sl4:D * (sl4 + 1)]
        em.wait(v, sX_ + str(sl4), value=16 * (gs // 4 + 1))
        if gs >= 4:
            em.wait(v, sRS_, value=gs - 3)
        n = em.inc(lambda xs=xs, sl4=sl4: v.reduce_sum(
            mean_c[:, sl4:sl4 + 1], xs, axis=AXX), sCVn, 1)
        em.wait(v, sCVn, value=n)
        em.inc(lambda sl4=sl4: v.tensor_scalar(
            mean_c[:, sl4:sl4 + 1], mean_c[:, sl4:sl4 + 1],
            1.0 / D, None, ALU.mult), sLN_, 1)

    def _ln_norm(v, gs, x_sub, xn_tok, mean_c, rstd_c, sems, trbase):
        sX_, sLN_, sRS_, sXN_, sTR_, sCVn = sems
        sl4 = gs % 4
        slot = gs % 2
        xs = x_sub[:, D * sl4:D * (sl4 + 1)]
        em.wait(v, sRS_, value=gs + 1)
        if gs >= 2:
            bt2, s2 = (gs - 2) // 4, (gs - 2) % 4
            em.wait(v, sTR_, value=trbase(bt2) + (s2 + 1) * DC)
        em.inc(lambda xs=xs, slot=slot, sl4=sl4: v.tensor_scalar(
            xn_tok[:, D * slot:D * (slot + 1)], xs,
            mean_c[:, sl4:sl4 + 1], rstd_c[:, sl4:sl4 + 1],
            ALU.subtract, ALU.mult), sXN_, 1)

    def _ln_act(a, gs, x_sub, sq_scr, mean_c, var_c, rstd_c, sems):
        sLN_, sRS_, sXN_ = sems
        sl4 = gs % 4
        xs = x_sub[:, D * sl4:D * (sl4 + 1)]
        em.wait(a, sLN_, value=gs + 1)
        n = em.inc(lambda xs=xs, sl4=sl4: a.activation(
            sq_scr[:, :], xs, AF.Square, bias=mean_c[:, sl4:sl4 + 1],
            scale=-1.0, accum_out=var_c[:, sl4:sl4 + 1]), "sCA", 1)
        em.wait(a, "sCA", value=n)
        if gs >= 4:
            em.wait(a, sXN_, value=gs - 3)
        n = em.inc(lambda sl4=sl4: a.activation(
            rstd_c[:, sl4:sl4 + 1], var_c[:, sl4:sl4 + 1], AF.Ln,
            bias=eps_col[:, :], scale=1.0 / D), "sCA", 1)
        em.wait(a, "sCA", value=n)
        em.inc(lambda sl4=sl4: a.activation(
            rstd_c[:, sl4:sl4 + 1], rstd_c[:, sl4:sl4 + 1], AF.Exp,
            bias=0.0, scale=-0.5), sRS_, 1)

    P1SEMS = ("sX", "sLN", "sRS", "sXN", "sTR", "sCV")

    def _p1_evac_tile(v, bt):
        for pi in range(12):
            gidx = bt * 12 + pi
            em.wait(v, "sPJ", value=gidx + 1)
            if gidx >= 2:
                em.wait(v, f"sST{gidx % 2}", value=16 * ((gidx - 2) // 2 + 1))
            sl = 512 * (gidx % 2)
            if pi < 4:
                ps = bf["ps_q0"] if (pi % 2 == 0) else bf["ps_q1"]
                em.inc(lambda ps=ps, sl=sl, pi=pi: v.tensor_scalar(
                    bf["ev_q"][:, sl:sl + 512], ps[:, :], bq_pc[:, pi:pi + 1],
                    None, ALU.add), "sEV", 1)
            elif pi < 8:
                ps = bf["ps_k0"] if (pi % 2 == 0) else bf["ps_k1"]
                em.inc(lambda ps=ps, sl=sl, pi=pi: v.tensor_scalar(
                    bf["ev_k"][:, sl:sl + 512], ps[:, :], bk_pc[:, pi - 4:pi - 3],
                    None, ALU.add), "sEV", 1)
            else:
                ps = bf["ps_v0"] if (pi % 2 == 0) else bf["ps_v1"]
                em.inc(lambda ps=ps, sl=sl: v.scalar_tensor_tensor(
                    bf["ev_v"][:, sl:sl + 512], ps[:, :], 1.0, bvb[:, :],
                    ALU.mult, ALU.add), "sEV", 1)

    def p1_dve(v):
        x_sub, xn_tok = bf["x_sub"], bf["xn_tok"]
        mean_c, rstd_c = bf["mean_c"], bf["rstd_c"]
        for bt in range(NBT):
            for s_ in range(4):
                _ln_mean(v, 4 * bt + s_, x_sub, mean_c, P1SEMS)
            for s_ in range(4):
                _ln_norm(v, 4 * bt + s_, x_sub, xn_tok, mean_c, rstd_c, P1SEMS,
                         lambda bt2: bt2 * 4 * DC)
            _p1_evac_tile(v, bt)

    def _p1_trevac_tile(a, bt):
        fslot = bt % 2
        if bt >= 2:
            em.wait(a, "sPJ", value=12 * (bt - 1))
        for tix in range(4 * DC):
            tr_abs = bt * 4 * DC + tix
            s_, c = tix // DC, tix % DC
            em.wait(a, "sTR", value=tr_abs + 1)
            ps = bf["ps_tr0"] if (tr_abs % 2 == 0) else bf["ps_tr1"]
            dst = bf["xn_fm"][:, (fslot * DC + c) * 512 + 128 * s_:
                              (fslot * DC + c) * 512 + 128 * (s_ + 1)]
            em.inc(lambda dst=dst, ps=ps: a.copy(dst, ps[:, 0:128]), "sTE", 1)

    def p1_act(a):
        x_sub = bf["x_sub"]
        for bt in range(NBT):
            if bt >= 1:
                _p1_trevac_tile(a, bt - 1)
            for s_ in range(4):
                _ln_act(a, 4 * bt + s_, x_sub, bf["sq_scr"], bf["mean_c"],
                        bf["var_c"], bf["rstd_c"], ("sLN", "sRS", "sXN"))
        _p1_trevac_tile(a, NBT - 1)

    def p1_tensor(t):
        xn_tok, xn_fm = bf["xn_tok"], bf["xn_fm"]
        em.wait(t, "sWD", "qkvT_done")
        for bt in range(NBT):
            fslot = bt % 2
            for tix in range(4 * DC):
                tr_abs = bt * 4 * DC + tix
                s_, c = tix // DC, tix % DC
                gs = 4 * bt + s_
                em.wait(t, "sXN", value=gs + 1)
                if tr_abs >= 2:
                    em.wait(t, "sTE", value=tr_abs - 1)
                ps = bf["ps_tr0"] if (tr_abs % 2 == 0) else bf["ps_tr1"]
                src = xn_tok[:, D * (gs % 2) + 128 * c:D * (gs % 2) + 128 * (c + 1)]
                em.inc(lambda ps=ps, src=src: t.transpose(ps[:, 0:128], src,
                                                          iden_bf[:, :]), "sTR", 1)
            em.wait(t, "sTE", value=(bt + 1) * 4 * DC)
            xf0 = fslot * DC * 512
            for pi in range(12):
                gidx = bt * 12 + pi
                if gidx >= 2:
                    em.wait(t, "sEV", value=gidx - 1)
                if pi < 4:
                    ps, wsb = (bf["ps_q0"] if pi % 2 == 0 else bf["ps_q1"]), bf["wqT_sb"]
                    lh = pi
                    stat = lambda dcx, wsb=wsb, lh=lh: wsb[:, 512 * dcx + 128 * lh:
                                                          512 * dcx + 128 * (lh + 1)]
                    mov = lambda dcx, xf0=xf0: xn_fm[:, xf0 + 512 * dcx:xf0 + 512 * (dcx + 1)]
                elif pi < 8:
                    ps, wsb = (bf["ps_k0"] if pi % 2 == 0 else bf["ps_k1"]), bf["wkT_sb"]
                    lh = pi - 4
                    stat = lambda dcx, wsb=wsb, lh=lh: wsb[:, 512 * dcx + 128 * lh:
                                                          512 * dcx + 128 * (lh + 1)]
                    mov = lambda dcx, xf0=xf0: xn_fm[:, xf0 + 512 * dcx:xf0 + 512 * (dcx + 1)]
                else:
                    ps = bf["ps_v0"] if pi % 2 == 0 else bf["ps_v1"]
                    s_ = pi - 8
                    stat = lambda dcx, xf0=xf0, s_=s_: xn_fm[:, xf0 + 512 * dcx + 128 * s_:
                                                             xf0 + 512 * dcx + 128 * (s_ + 1)]
                    mov = lambda dcx: bf["wvT_sb"][:, 512 * dcx:512 * (dcx + 1)]
                for dcx in range(DC - 1):
                    em.do(lambda ps=ps, stat=stat, mov=mov, dcx=dcx: t.matmul(
                        ps[:, :], stat(dcx), mov(dcx),
                        start=(dcx == 0), stop=False))
                em.inc(lambda ps=ps, stat=stat, mov=mov: t.matmul(
                    ps[:, :], stat(DC - 1), mov(DC - 1),
                    start=False, stop=True), "sPJ", 1)

    def p2_sync(s):
        pending = []
        next_prep = [0]

        def _prep_steps(s, ui):
            if part != 0:
                return
            target = min(NPREP, (ui + 1) * NPREP // 56)
            while pending and pending[0][0] is not None:
                _prep_store(s, pending.pop(0))
            while next_prep[0] < target:
                pending.append(_prep_item(s, next_prep[0]))
                next_prep[0] += 1

        def _av_store(s, ui):
            u = units[ui]
            j = u["j"]
            slot = ui % 2
            w = u["w"]
            em.wait(s, "sAB", value=ui + 1)
            qq = u["qt0"] // 2048
            qoff = u["qt0"] % 2048
            em.inc(lambda j=j, qq=qq, qoff=qoff, slot=slot, w=w: s.dma_start(
                out=av_local.ap()[512 * qq + 128 * j:512 * qq + 128 * (j + 1),
                                  qoff:qoff + w],
                in_=bf["av_ev"][:, 512 * slot:512 * slot + w]), f"sAO{ui % 2}", 16)

        for ui, u in enumerate(units):
            slot = ui % 2
            j, t_ = u["j"], u["t"]
            if ui >= 2:
                em.wait(s, "sBC", value=ui - 1)
            c0 = 128 * u["t_start"]
            c1 = 128 * (u["t_start"] + u["nkt"])
            em.inc(lambda j=j, c0=c0, c1=c1, slot=slot: s.dma_start(
                out=bf["k_str"][:, S * slot:S * slot + (c1 - c0)],
                in_=kbuf.ap()[j, :, c0:c1]), f"sKV{ui % 2}", 16)
            em.inc(lambda j=j, u=u, slot=slot, c0=c0, c1=c1: s.dma_start(
                out=bf["v_str"][:, S * slot:S * slot + (c1 - c0)].rearrange(
                    "p (T h) -> p T h", h=128),
                in_=vbuf.ap()[j, u["t_start"]:u["t_start"] + u["nkt"], :, :].rearrange(
                    "T p h -> p T h")), f"sKV{ui % 2}", 16)
            w = u["w"]
            em.inc(lambda j=j, u=u, slot=slot, w=w: s.dma_start(
                out=bf["q_str"][:, 512 * slot:512 * slot + w],
                in_=qbuf.ap()[j, :, u["qt0"]:u["qt0"] + w]), f"sKV{ui % 2}", 16)
            _prep_steps(s, ui)
            if ui >= 1:
                _av_store(s, ui - 1)
        _av_store(s, len(units) - 1)
        while pending:
            _prep_store(s, pending.pop(0))
        if part == 0:
            em.mark("sPR0", "prep_done0")
            em.mark("sPR1", "prep_done1")
        em.mark("sAO0", "p2_av0")
        em.mark("sAO1", "p2_av1")
        em.do(lambda: s.wait_ge(em.handles["sAO0"], em.marks["p2_av0"]))
        em.do(lambda: s.wait_ge(em.handles["sAO1"], em.marks["p2_av1"]))

    def _p2_avl(t, u, idx, vs):
        tl = u["tiles"][idx]
        gi = tl["gi"]
        w = u["w"]
        sl = 512 * (gi % 2)
        if tl["diag"]:
            em.wait(t, "sM_", value=tl["m"])
            psrc = bf["pm_pp"][:, sl:sl + w]
        else:
            em.wait(t, "sP_", value=gi + 1)
            psrc = bf["p_pp"][:, sl:sl + w]
        if tl["first"] and tl["g"] > 0:
            em.wait(t, "sR_", label=f"resc_{u['ui']}_{tl['g'] - 1}")
        em.do(lambda psrc=psrc, tl=tl, w=w: t.matmul(
            bf["ps_l"][:, 0:w], ones_bf[:, :], psrc, start=tl["first"], stop=tl["last"]))
        em.inc(lambda psrc=psrc, tl=tl, vs=vs, idx=idx, w=w: t.matmul(
            bf["ps_av"][:, 0:w], vs[:, 128 * idx:128 * (idx + 1)], psrc,
            start=tl["first"], stop=tl["last"]), "sAV", 1)

    def p2_tensor(t):
        for ui, u in enumerate(units):
            slot = ui % 2
            em.wait(t, f"sKV{ui % 2}", value=48 * (ui // 2 + 1))
            w = u["w"]
            ks = bf["k_str"][:, S * slot:S * slot + S]
            vs = bf["v_str"][:, S * slot:S * slot + S]
            qs = bf["q_str"][:, 512 * slot:512 * slot + w]
            ntiles = u["nkt"]
            for idx in range(ntiles):
                tl = u["tiles"][idx]
                gi = tl["gi"]
                if gi >= 2:
                    em.wait(t, "sP_", value=gi - 1)
                ps_s = bf["ps_s0"] if (gi % 2 == 0) else bf["ps_s1"]
                em.inc(lambda ps_s=ps_s, ks=ks, qs=qs, idx=idx, w=w: t.matmul(
                    ps_s[:, 0:w], ks[:, 128 * idx:128 * (idx + 1)], qs,
                    start=True, stop=True), "sS_", 1)
                if idx > 0:
                    _p2_avl(t, u, idx - 1, vs)
            _p2_avl(t, u, ntiles - 1, vs)
            em.wait(t, "sAB", value=ui)
            em.wait(t, "sRL", value=ui + 1)
            em.inc(lambda w=w: t.matmul(bf["ps_bc"][:, 0:w], ones_row_f[:, :],
                                        bf["rl_sb"][:, 0:w],
                                        start=True, stop=True), "sBC", 1)

    def p2_act(a):
        for ui, u in enumerate(units):
            j = u["j"]
            w = u["w"]
            for tl in u["tiles"]:
                gi = tl["gi"]
                em.wait(a, "sS_", value=gi + 1)
                if gi >= 2:
                    ptl = all_tiles[gi - 2]
                    if ptl["diag"]:
                        em.wait(a, "sM_", value=ptl["m"])
                    else:
                        em.wait(a, "sAV", value=gi - 1)
                ps_s = bf["ps_s0"] if (gi % 2 == 0) else bf["ps_s1"]
                if tl["psz"]:
                    col = pcol[(j, tl["psz"])] + tl["jg"]
                else:
                    col = bias_off[j] + tl["jg"]
                em.inc(lambda gi=gi, ps_s=ps_s, col=col, w=w: a.activation(
                    bf["p_pp"][:, 512 * (gi % 2):512 * (gi % 2) + w], ps_s[:, 0:w],
                    AF.Exp, bias=biastab[:, col:col + 1], scale=SCALE), "sP_", 1)

    def p2_gp(g):
        for di, tl in enumerate(diag_tiles):
            gi = tl["gi"]
            em.wait(g, "sP_", value=gi + 1)
            if di >= 2:
                em.wait(g, "sAV", value=diag_tiles[di - 2]["gi"] + 1)
            u = tl["_u"]
            w = u["w"]
            em.inc(lambda gi=gi, u=u, tl=tl, w=w: g.affine_select(
                bf["pm_pp"][:, 512 * (gi % 2):512 * (gi % 2) + w],
                bf["p_pp"][:, 512 * (gi % 2):512 * (gi % 2) + w],
                [[1, w]], ALU.is_ge, 0.0,
                base=u["qt0"] - 128 * tl["T"], channel_multiplier=-1), "sM_", 1)

    def p2_dve(v):
        acc_av, acc_l, rl_sb = bf["acc_av"], bf["acc_l"], bf["rl_sb"]
        for ui, u in enumerate(units):
            j = u["j"]
            w = u["w"]
            glast = {}
            for tl in u["tiles"]:
                if tl["last"]:
                    glast[tl["g"]] = tl["i"]
            for g in range(u["ng"]):
                last_i = glast[g]
                em.wait(v, "sAV", value=u["tiles"][last_i]["gi"] + 1)
                if g == 0:
                    if ui >= 1:
                        em.wait(v, "sAB", value=ui)
                    em.inc(lambda w=w: v.tensor_copy(acc_av[:, 0:w], bf["ps_av"][:, 0:w]),
                           "sR_", 1)
                    em.inc(lambda w=w: v.tensor_copy(acc_l[:, 0:w], bf["ps_l"][:, 0:w]),
                           "sR_", 1, f"resc_{ui}_0")
                else:
                    rj = prcol[(j, u["sz0"])] if (g == 1 and u["sz0"]) else j
                    rc = rtab[:, rj:rj + 1]
                    r1 = rtab[0:1, rj:rj + 1]
                    n = em.inc(lambda rc=rc, w=w: v.tensor_scalar(
                        acc_av[:, 0:w], acc_av[:, 0:w], rc, None, ALU.mult), "sR_", 1)
                    em.wait(v, "sR_", value=n)
                    em.inc(lambda rc=rc, w=w: v.scalar_tensor_tensor(
                        acc_av[:, 0:w], acc_av[:, 0:w], rc, bf["ps_av"][:, 0:w],
                        ALU.mult, ALU.add), "sR_", 1)
                    n = em.inc(lambda r1=r1, w=w: v.tensor_scalar(
                        acc_l[:, 0:w], acc_l[:, 0:w], r1, None, ALU.mult), "sR_", 1)
                    em.wait(v, "sR_", value=n)
                    em.inc(lambda r1=r1, w=w: v.scalar_tensor_tensor(
                        acc_l[:, 0:w], acc_l[:, 0:w], r1, bf["ps_l"][:, 0:w],
                        ALU.mult, ALU.add), "sR_", 1, f"resc_{ui}_{g}")
            em.wait(v, "sR_", value=em.cnt["sR_"])
            em.inc(lambda w=w: v.reciprocal(rl_sb[:, 0:w], acc_l[:, 0:w]), "sRL", 1)
            em.wait(v, "sBC", value=ui + 1)
            if ui >= 2:
                em.wait(v, f"sAO{ui % 2}", value=16 * ((ui - 2) // 2 + 1))
            em.inc(lambda ui=ui, w=w: v.tensor_tensor(
                bf["av_ev"][:, 512 * (ui % 2):512 * (ui % 2) + w], acc_av[:, 0:w],
                bf["ps_bc"][:, 0:w], ALU.mult), "sAB", 1)

    N3GS = 4 * NOT_

    def _ugw_lbl(ot, oc, mi):
        return f"w_ug_{ot}_{oc}_{mi}"

    def _dnw_lbl(ot, oc):
        return f"w_dn_{ot}_{oc}"

    def prep_standalone(s):
        em.wait(s, "sCast", "casts_done")
        pend = None
        for kk in range(NPREP):
            if pend is not None:
                _prep_store(s, pend)
            pend = _prep_item(s, kk)
        _prep_store(s, pend)
        em.mark("sPR0", "prep_done0")
        em.mark("sPR1", "prep_done1")
        em.do(lambda: s.wait_ge(em.handles["sPR0"], em.marks["prep_done0"]))
        em.do(lambda: s.wait_ge(em.handles["sPR1"], em.marks["prep_done1"]))

    def p3_gp(g):
        n = em.inc(lambda: g.dma_start(
            out=a2a_in.ap()[0:OWN, :], in_=av_local.ap()), "sCC", 16)
        n = em.inc(lambda: g.dma_start(
            out=a2a_in.ap()[OWN:2 * OWN, :], in_=av_local.ap()), "sCC", 16)
        em.do(lambda n=n: g.wait_ge(em.handles["sCC"], n))

        def _cc():
            with allow_ici_all_to_all():
                return g.collective_compute(
                    "AllToAll", ALU.bypass, replica_groups=[list(range(8))],
                    ins=[a2a_in.ap().opt()], outs=[a2a_out.ap().opt()])
        n = em.inc(_cc, "sCC2", 16)
        em.do(lambda n=n: g.wait_ge(em.handles["sCC2"], n))

    def _p3_store_tile(s, ot):
        for oc in range(DC):
            gidx = ot * DC + oc
            em.wait(s, "s3FF", value=gidx + 1)
            em.inc(lambda ot=ot, oc=oc, gidx=gidx: s.dma_start(
                out=ffnfm.ap()[128 * oc:128 * (oc + 1), 512 * ot:512 * (ot + 1)],
                in_=bf["ffn_ev"][:, 512 * (gidx % 2):512 * (gidx % 2 + 1)]),
                f"s3ST{gidx % 2}", 16)

    def p3_sync(s):
        for ot in range(NOT_):
            for s_ in range(4):
                gs = 4 * ot + s_
                slot = gs % 2
                if gs >= 4:
                    em.wait(s, "s3XN", value=gs - 3)
                em.inc(lambda gs=gs: s.dma_start(
                    out=bf["x3_sub"][:, D * (gs % 4):D * (gs % 4 + 1)],
                    in_=x_own[128 * gs:128 * (gs + 1), :]), f"s3X{gs % 4}", 16)
            if ot >= 1:
                _p3_store_tile(s, ot - 1)
        _p3_store_tile(s, NOT_ - 1)
        em.mark("s3ST0", "p3_stores0")
        em.mark("s3ST1", "p3_stores1")
        em.do(lambda: s.wait_ge(em.handles["s3ST0"], em.marks["p3_stores0"]))
        em.do(lambda: s.wait_ge(em.handles["s3ST1"], em.marks["p3_stores1"]))

    def p3_scalar(a):
        wst, wstd = bf["wst"], bf["wstd"]

        def ugload(ot, oc, mi):
            kk = (ot * IC + oc) * 2 + mi
            slot = kk % 4
            wsrc = wgateT if mi == 0 else wupT
            if kk >= 4:
                kp = kk - 4
                oc2, mi2 = kp // 2 % IC, kp % 2
                ot2 = kp // (2 * IC)
                em.wait(a, "s3PE", label=("gate" if mi2 == 0 else "up") +
                        f"_{ot2}_{oc2}")
            em.inc(lambda wsrc=wsrc, oc=oc, slot=slot: a.dma_start(
                out=wst[:, DC * 128 * slot:DC * 128 * (slot + 1)].rearrange(
                    "p (c o) -> p c o", c=DC),
                in_=bass.AP(wsrc, 128 * oc,
                            [[INTER, 128], [INTER * 128, DC], [1, 128]])),
                f"s3W{kk % 4}", 16)

        def dnload(ot, oc):
            kd = ot * DC + oc
            slot = kd % 2
            if kd >= 2:
                em.wait(a, "s3PE", label=f"dnmm_{(kd - 2) // DC}_{(kd - 2) % DC}")
            em.inc(lambda oc=oc, slot=slot: a.dma_start(
                out=wstd[:, IC * 128 * slot:IC * 128 * (slot + 1)].rearrange(
                    "p (c o) -> p c o", c=IC),
                in_=bass.AP(wdownT, 128 * oc,
                            [[D, 128], [D * 128, IC], [1, 128]])),
                f"s3WD{kd % 2}", 16)

        def evacs_tile(ot):
            if ot >= 1:
                em.wait(a, "s3PE", label=f"suh_{ot - 1}")
            for tix in range(4 * DC):
                tr_abs = ot * 4 * DC + tix
                s_, c = tix // DC, tix % DC
                em.wait(a, "s3TR", value=tr_abs + 1)
                ps = bf["ps_tr0"] if (tr_abs % 2 == 0) else bf["ps_tr1"]
                dst = bf["t1_sb"][:, c * 512 + 128 * s_:c * 512 + 128 * (s_ + 1)]
                em.inc(lambda dst=dst, ps=ps: a.copy(dst, ps[:, 0:128]), "s3TE", 1)

        def wsil_tile(ot):
            ugload(ot, 0, 0)
            ugload(ot, 0, 1)
            ugload(ot, 1, 0)
            ugload(ot, 1, 1)
            for oc in range(IC):
                if oc + 2 < IC:
                    ugload(ot, oc + 2, 0)
                    ugload(ot, oc + 2, 1)
                em.wait(a, "s3PE", label=f"gate_{ot}_{oc}")
                if oc >= 1:
                    em.wait(a, "s3H", label=f"hst_{ot}_{oc - 1}")
                em.inc(lambda oc=oc: a.activation(
                    bf["sil_sb"][:, :], bf["ps3_a"][:, :], AF.Silu,
                    bias=0.0, scale=wsg_pc[:, oc:oc + 1]), "s3SIL", 1)
            for oc in range(DC):
                dnload(ot, oc)

        em.wait(a, "sPR0", "prep_done0")
        em.wait(a, "sPR1", "prep_done1")
        for ot in range(NOT_):
            if ot >= 1:
                evacs_tile(ot - 1)
            for s_ in range(4):
                _ln_act(a, 4 * ot + s_, bf["x3_sub"], bf["sq3"], bf["mean3"],
                        bf["var3"], bf["rstd3"], ("s3LN", "s3RS", "s3XN"))
            if ot >= 1:
                wsil_tile(ot - 1)
        evacs_tile(NOT_ - 1)
        wsil_tile(NOT_ - 1)

    def _silcnt(ot, oc):
        return ot * IC + oc + 1

    def p3_dve(v):
        h_sb, t1_sb, t2_sb = bf["h_sb"], bf["t1_sb"], bf["t2_sb"]
        P3SEMS = ("s3X", "s3LN", "s3RS", "s3XN", "s3TR", "s3V")

        def tilework(ot):
            for c in range(DC):
                em.wait(v, "s3PE", label=f"suhmm_{ot}_{c}")
                ps = bf["ps3_a"] if (c % 2 == 0) else bf["ps3_b"]
                em.inc(lambda c=c, ps=ps: v.tensor_copy(
                    t2_sb[:, 512 * c:512 * (c + 1)], ps[:, :]), "s3V", 1,
                    f"suhev_{ot}_{c}")
            em.wait(v, "s3PE", label=f"suh_{ot}")
            em.wait(v, "s3V", value=em.cnt["s3V"])
            cur_is_t2 = True
            for st in range(4):
                bit = 1 << st
                cur = t2_sb if cur_is_t2 else t1_sb
                nxt = t1_sb if cur_is_t2 else t2_sb
                for c in range(DC):
                    a_ = c & ~bit
                    b_ = a_ | bit
                    sa = cur[:, 512 * a_:512 * (a_ + 1)]
                    sbb = cur[:, 512 * b_:512 * (b_ + 1)]
                    dst = nxt[:, 512 * c:512 * (c + 1)]
                    op = ALU.subtract if (c & bit) else ALU.add
                    em.inc(lambda dst=dst, sa=sa, sbb=sbb, op=op: v.tensor_tensor(
                        dst, sa, sbb, op), "s3V", 1)
                em.wait(v, "s3V", value=em.cnt["s3V"])
                cur_is_t2 = not cur_is_t2
            em.mark("s3V", f"t2fin_{ot}")
            for oc in range(IC):
                em.wait(v, "s3SIL", value=_silcnt(ot, oc))
                em.wait(v, "s3PE", label=f"up_{ot}_{oc}")
                if ot > 0 and oc == 0:
                    em.wait(v, "s3PE", label=f"dnmm_{ot - 1}_{DC - 1}")
                em.inc(lambda oc=oc: v.scalar_tensor_tensor(
                    h_sb[:, 512 * oc:512 * (oc + 1)], bf["ps3_b"][:, :],
                    wsu_eff[:, oc:oc + 1], bf["sil_sb"][:, :], ALU.mult, ALU.mult),
                    "s3H", 1, f"hst_{ot}_{oc}")
            for c in range(IC):
                em.wait(v, "s3PE", label=f"f2_{ot}_{c}")
                ps = bf["ps3_a"] if (c % 2 == 0) else bf["ps3_b"]
                em.inc(lambda c=c, ps=ps: v.tensor_copy(
                    h_sb[:, 512 * c:512 * (c + 1)], ps[:, :]), "s3V", 1,
                    f"f2ev_{ot}_{c}")
            em.wait(v, "s3V", value=em.cnt["s3V"])
            for st in range(6):
                bit = 1 << st
                pairs = [(c, c | bit) for c in range(IC) if not (c & bit)]
                for a_, b_ in pairs:
                    em.inc(lambda a_=a_, b_=b_: v.tensor_tensor(
                        h_sb[:, 512 * b_:512 * (b_ + 1)],
                        h_sb[:, 512 * a_:512 * (a_ + 1)],
                        h_sb[:, 512 * b_:512 * (b_ + 1)], ALU.subtract), "s3V", 1)
                em.wait(v, "s3V", value=em.cnt["s3V"])
                for a_, b_ in pairs:
                    em.inc(lambda a_=a_, b_=b_: v.scalar_tensor_tensor(
                        h_sb[:, 512 * a_:512 * (a_ + 1)],
                        h_sb[:, 512 * a_:512 * (a_ + 1)], 2.0,
                        h_sb[:, 512 * b_:512 * (b_ + 1)], ALU.mult, ALU.subtract),
                        "s3V", 1)
                em.wait(v, "s3V", value=em.cnt["s3V"])
            em.mark("s3V", f"bfly_{ot}")
            for oc in range(DC):
                gidx = ot * DC + oc
                em.wait(v, "s3PE", label=f"dnmm_{ot}_{oc}")
                if gidx >= 2:
                    em.wait(v, f"s3ST{gidx % 2}", value=16 * ((gidx - 2) // 2 + 1))
                em.inc(lambda oc=oc, gidx=gidx: v.tensor_scalar(
                    bf["ffn_ev"][:, 512 * (gidx % 2):512 * (gidx % 2 + 1)],
                    bf["ps3_c"][:, :], wsdp_pc[:, oc:oc + 1], None, ALU.mult),
                    "s3FF", 1)

        for ot in range(NOT_):
            for s_ in range(4):
                _ln_mean(v, 4 * ot + s_, bf["x3_sub"], bf["mean3"], P3SEMS)
            if ot >= 1:
                tilework(ot - 1)
            for s_ in range(4):
                _ln_norm(v, 4 * ot + s_, bf["x3_sub"], bf["xn3_tok"], bf["mean3"],
                         bf["rstd3"], P3SEMS, lambda ot2: ot2 * 4 * DC)
        tilework(NOT_ - 1)

    def p3_tensor(t):
        t1_sb, t2_sb, h_sb = bf["t1_sb"], bf["t2_sb"], bf["h_sb"]
        wst, wstd = bf["wst"], bf["wstd"]
        for ot in range(NOT_):
            for tix in range(4 * DC):
                tr_abs = ot * 4 * DC + tix
                s_, c = tix // DC, tix % DC
                gs = 4 * ot + s_
                em.wait(t, "s3XN", value=gs + 1)
                if tr_abs >= 2:
                    em.wait(t, "s3TE", value=tr_abs - 1)
                ps = bf["ps_tr0"] if (tr_abs % 2 == 0) else bf["ps_tr1"]
                src = bf["xn3_tok"][:, D * (gs % 2) + 128 * c:D * (gs % 2) + 128 * (c + 1)]
                em.inc(lambda ps=ps, src=src: t.transpose(ps[:, 0:128], src,
                                                          iden_bf[:, :]), "s3TR", 1)
            em.wait(t, "s3TE", value=(ot + 1) * 4 * DC)
            for c in range(DC):
                ps = bf["ps3_a"] if (c % 2 == 0) else bf["ps3_b"]
                if c >= 2:
                    em.wait(t, "s3V", label=f"suhev_{ot}_{c - 2}")
                em.inc(lambda c=c, ps=ps: t.matmul(
                    ps[:, :], suh_ug[:, 128 * c:128 * (c + 1)],
                    t1_sb[:, 512 * c:512 * (c + 1)], start=True, stop=True),
                    "s3PE", 1, f"suhmm_{ot}_{c}")
            em.mark("s3PE", f"suh_{ot}")
            em.wait(t, "s3V", label=f"t2fin_{ot}")
            for oc in range(IC):
                for mi in range(2):
                    kkw = (ot * IC + oc) * 2 + mi
                    em.wait(t, f"s3W{kkw % 4}", value=16 * (kkw // 4 + 1))
                    if mi == 0:
                        ps = bf["ps3_a"]
                        if oc >= 1:
                            em.wait(t, "s3SIL", value=_silcnt(ot, oc - 1))
                    else:
                        ps = bf["ps3_b"]
                        if oc >= 1:
                            em.wait(t, "s3H", label=f"hst_{ot}_{oc - 1}")
                    kk = (ot * IC + oc) * 2 + mi
                    slot = kk % 4
                    for dcx in range(DC - 1):
                        em.do(lambda ps=ps, slot=slot, dcx=dcx, wst=wst: t.matmul(
                            ps[:, :],
                            wst[:, DC * 128 * slot + 128 * dcx:
                                DC * 128 * slot + 128 * (dcx + 1)],
                            t2_sb[:, 512 * dcx:512 * (dcx + 1)],
                            start=(dcx == 0), stop=False))
                    em.inc(lambda ps=ps, slot=slot, wst=wst: t.matmul(
                        ps[:, :],
                        wst[:, DC * 128 * slot + 128 * (DC - 1):DC * 128 * (slot + 1)],
                        t2_sb[:, 512 * (DC - 1):512 * DC],
                        start=False, stop=True), "s3PE", 1,
                        (f"gate_{ot}_{oc}" if mi == 0 else f"up_{ot}_{oc}"))
            em.wait(t, "s3H", label=f"hst_{ot}_{IC - 1}")
            for c in range(IC):
                ps = bf["ps3_a"] if (c % 2 == 0) else bf["ps3_b"]
                if c >= 2:
                    em.wait(t, "s3V", label=f"f2ev_{ot}_{c - 2}")
                em.inc(lambda c=c, ps=ps: t.matmul(
                    ps[:, :], h128n[:, :], h_sb[:, 512 * c:512 * (c + 1)],
                    start=True, stop=True), "s3PE", 1, f"f2_{ot}_{c}")
            em.wait(t, "s3V", label=f"bfly_{ot}")
            for oc in range(DC):
                kd = ot * DC + oc
                em.wait(t, f"s3WD{kd % 2}", value=16 * (kd // 2 + 1))
                if kd >= 1:
                    em.wait(t, "s3FF", value=kd)
                slot = kd % 2
                for dcx in range(IC - 1):
                    em.do(lambda slot=slot, dcx=dcx, wstd=wstd: t.matmul(
                        bf["ps3_c"][:, :],
                        wstd[:, IC * 128 * slot + 128 * dcx:
                             IC * 128 * slot + 128 * (dcx + 1)],
                        h_sb[:, 512 * dcx:512 * (dcx + 1)],
                        start=(dcx == 0), stop=False))
                em.inc(lambda slot=slot, wstd=wstd: t.matmul(
                    bf["ps3_c"][:, :],
                    wstd[:, IC * 128 * slot + 128 * (IC - 1):IC * 128 * (slot + 1)],
                    h_sb[:, 512 * (IC - 1):512 * IC],
                    start=False, stop=True), "s3PE", 1, f"dnmm_{ot}_{oc}")

    def p4_sync(s):
        avsrc_lo = av_in if part == 2 else a2a_out
        for ci in range(DC):
            r0 = 512 * (ci % 4) + 128 * (ci // 4)
            if ci >= 2:
                em.wait(s, "s4EV", label=f"p4cmb_{ci - 2}")
            em.inc(lambda ci=ci, r0=r0, avsrc_lo=avsrc_lo: s.dma_start(
                out=bf["av_sb"][:, 2048 * ci:2048 * (ci + 1)],
                in_=avsrc_lo.ap()[r0:r0 + 128, :]), f"s4W{ci % 2}", 16)
            hiap = (avsrc_lo.ap()[r0:r0 + 128, :] if part == 2
                    else a2a_out.ap()[OWN + r0:OWN + r0 + 128, :])
            em.inc(lambda ci=ci, hiap=hiap: s.dma_start(
                out=bf["avtmp"][:, 2048 * (ci % 2):2048 * (ci % 2 + 1)],
                in_=hiap), f"s4H{ci % 2}", 16)
        for tb in range(DC):
            slot = tb % 2
            if tb >= 2:
                em.wait(s, "s4F", value=tb - 1)
            em.inc(lambda tb=tb, slot=slot: s.dma_start(
                out=bf["ff4_sb"][:, D * slot:D * (slot + 1)],
                in_=ffnfm.ap()[:, 128 * tb:128 * (tb + 1)], transpose=True),
                f"s4X{tb % 2}", 16)
            em.inc(lambda tb=tb, slot=slot: s.dma_start(
                out=bf["x4_sb"][:, D * slot:D * (slot + 1)],
                in_=x_own[128 * tb:128 * (tb + 1), :]), f"s4X{tb % 2}", 16)
            em.wait(s, "s4F", value=tb + 1)
            em.inc(lambda tb=tb, slot=slot: s.dma_start(
                out=y_own[128 * tb:128 * (tb + 1), :],
                in_=bf["fin_sb"][:, D * slot:D * (slot + 1)]), f"s4ST{tb % 2}", 16)
        em.mark("s4ST0", "p4_stores0")
        em.mark("s4ST1", "p4_stores1")
        em.do(lambda: s.wait_ge(em.handles["s4ST0"], em.marks["p4_stores0"]))
        em.do(lambda: s.wait_ge(em.handles["s4ST1"], em.marks["p4_stores1"]))

    def p4_scalar(a):
        for ci in range(DC):
            em.inc(lambda ci=ci: a.dma_start(
                out=bf["owT_sb"][:, 2048 * ci:2048 * (ci + 1)],
                in_=outw_bf.ap()[:, 128 * ci:128 * (ci + 1)], transpose=True),
                "s4WO", 16)
        em.mark("s4WO", "p4_w")

    def p4_dve_combine(v):
        for ci in range(DC):
            em.wait(v, f"s4W{ci % 2}", value=16 * (ci // 2 + 1))
            em.wait(v, f"s4H{ci % 2}", value=16 * (ci // 2 + 1))
            n = em.inc(lambda ci=ci: v.tensor_scalar(
                bf["av_sb"][:, 2048 * ci:2048 * (ci + 1)],
                bf["av_sb"][:, 2048 * ci:2048 * (ci + 1)],
                bm0_sb[:, :], None, ALU.mult), "s4EV", 1)
            em.wait(v, "s4EV", value=n)
            em.inc(lambda ci=ci: v.scalar_tensor_tensor(
                bf["av_sb"][:, 2048 * ci:2048 * (ci + 1)],
                bf["avtmp"][:, 2048 * (ci % 2):2048 * (ci % 2 + 1)], bm1_sb[:, :],
                bf["av_sb"][:, 2048 * ci:2048 * (ci + 1)],
                ALU.mult, ALU.add), "s4EV", 1, f"p4cmb_{ci}")
        em.mark("s4EV", "p4_avcomb")

    def p4_tensor(t):
        em.wait(t, "s4WO", "p4_w")
        em.wait(t, "s4EV", "p4_avcomb")
        for tb in range(DC):
            for os_ in range(4):
                k_ = tb * 4 + os_
                ps = bf["ps4_0"] if (k_ % 2 == 0) else bf["ps4_1"]
                if k_ >= 2:
                    kp = k_ - 2
                    em.wait(t, "s4EV", label=f"p4ev_{kp // 4}_{kp % 4}")
                for ci in range(DC - 1):
                    em.do(lambda ps=ps, ci=ci, tb=tb, os_=os_: t.matmul(
                        ps[:, :],
                        bf["av_sb"][:, 2048 * ci + 128 * tb:2048 * ci + 128 * (tb + 1)],
                        bf["owT_sb"][:, 2048 * ci + 512 * os_:2048 * ci + 512 * (os_ + 1)],
                        start=(ci == 0), stop=False))
                em.inc(lambda ps=ps, tb=tb, os_=os_: t.matmul(
                    ps[:, :],
                    bf["av_sb"][:, 2048 * (DC - 1) + 128 * tb:
                                2048 * (DC - 1) + 128 * (tb + 1)],
                    bf["owT_sb"][:, 2048 * (DC - 1) + 512 * os_:
                                 2048 * (DC - 1) + 512 * (os_ + 1)],
                    start=False, stop=True), "s4PE", 1)

    def p4_dve(v):
        for tb in range(DC):
            slot = tb % 2
            if tb >= 2:
                em.wait(v, f"s4ST{tb % 2}", value=16 * ((tb - 2) // 2 + 1))
            for os_ in range(4):
                k_ = tb * 4 + os_
                em.wait(v, "s4PE", value=k_ + 1)
                ps = bf["ps4_0"] if (k_ % 2 == 0) else bf["ps4_1"]
                em.inc(lambda ps=ps, slot=slot, os_=os_: v.scalar_tensor_tensor(
                    bf["fin_sb"][:, D * slot + 512 * os_:D * slot + 512 * (os_ + 1)],
                    ps[:, :], 1.0, outbb[:, 512 * os_:512 * (os_ + 1)],
                    ALU.mult, ALU.add), "s4EV", 1, f"p4ev_{tb}_{os_}")
            em.wait(v, f"s4X{tb % 2}", value=32 * (tb // 2 + 1))
            em.wait(v, "s4EV", value=em.cnt["s4EV"])
            n = em.inc(lambda slot=slot: v.tensor_tensor(
                bf["fin_sb"][:, D * slot:D * (slot + 1)],
                bf["fin_sb"][:, D * slot:D * (slot + 1)],
                bf["ff4_sb"][:, D * slot:D * (slot + 1)], ALU.add), "s4EV", 1)
            em.wait(v, "s4EV", value=n)
            em.inc(lambda slot=slot: v.tensor_tensor(
                bf["fin_sb"][:, D * slot:D * (slot + 1)],
                bf["fin_sb"][:, D * slot:D * (slot + 1)],
                bf["x4_sb"][:, D * slot:D * (slot + 1)], ALU.add), "s4F", 1)

    def alloc_p0(st):
        bf["bv_row"] = sb("bv_row", [1, NHL * HD], F32, st)
        bf["outb_row"] = sb("outb_row", [1, D], F32, st)
        bf["slope_row"] = sb("slope_row", [1, NHL], F32, st)
        bf["ps_cb"] = psum("ps_cb", [128, 512], F32, st)

    def alloc_p1(st):
        bf["x_sub"] = sb("x_sub", [128, 4 * D], F32, st)
        bf["xn_tok"] = sb("xn_tok", [128, 2 * D], BF16, st)
        bf["sq_scr"] = sb("sq_scr", [128, D], BF16, st)
        bf["mean_c"] = sb("mean_c", [128, 4], F32, st)
        bf["var_c"] = sb("var_c", [128, 4], F32, st)
        bf["rstd_c"] = sb("rstd_c", [128, 4], F32, st)
        bf["xn_fm"] = sb("xn_fm", [128, 2 * DC * 512], BF16, st)
        bf["wqT_sb"] = sb("wqT_sb", [128, DC * 512], BF16, st)
        bf["wkT_sb"] = sb("wkT_sb", [128, DC * 512], BF16, st)
        bf["wvT_sb"] = sb("wvT_sb", [128, DC * 512], BF16, st)
        bf["ev_q"] = sb("ev_q", [128, 2 * 512], BF16, st)
        bf["ev_k"] = sb("ev_k", [128, 2 * 512], BF16, st)
        bf["ev_v"] = sb("ev_v", [128, 2 * 512], BF16, st)
        bf["ps_tr0"] = psum("ps_tr0", [128, 1024], BF16, st)
        bf["ps_tr1"] = psum("ps_tr1", [128, 1024], BF16, st)
        bf["ps_q0"] = psum("ps_q0", [128, 512], F32, st)
        bf["ps_q1"] = psum("ps_q1", [128, 512], F32, st)
        bf["ps_k0"] = psum("ps_k0", [128, 512], F32, st)
        bf["ps_k1"] = psum("ps_k1", [128, 512], F32, st)
        bf["ps_v0"] = psum("ps_v0", [128, 512], F32, st)
        bf["ps_v1"] = psum("ps_v1", [128, 512], F32, st)

    def alloc_p2(st):
        bf["k_str"] = sb("k_str", [128, 2 * S], BF16, st)
        bf["v_str"] = sb("v_str", [128, 2 * S], BF16, st)
        bf["q_str"] = sb("q_str", [128, 2 * 512], BF16, st)
        bf["p_pp"] = sb("p_pp", [128, 2 * 512], BF16, st)
        bf["pm_pp"] = sb("pm_pp", [128, 2 * 512], BF16, st)
        bf["acc_av"] = sb("acc_av", [128, 512], F32, st)
        bf["acc_l"] = sb("acc_l", [1, 512], F32, st)
        bf["rl_sb"] = sb("rl_sb", [1, 512], F32, st)
        bf["av_ev"] = sb("av_ev", [128, 2 * 512], BF16, st)
        bf["wprep"] = sb("wprep", [128, 2 * INTER], BF16, st)
        bf["ps_s0"] = psum("ps_s0", [128, 512], F32, st)
        bf["ps_s1"] = psum("ps_s1", [128, 512], F32, st)
        bf["ps_av"] = psum("ps_av", [128, 512], F32, st)
        bf["ps_l"] = psum("ps_l", [1, 512], F32, st)
        bf["ps_bc"] = psum("ps_bc", [128, 512], F32, st)

    def alloc_p3(st):
        bf["x3_sub"] = sb("x3_sub", [128, 4 * D], F32, st)
        bf["xn3_tok"] = sb("xn3_tok", [128, 2 * D], BF16, st)
        bf["sq3"] = sb("sq3", [128, D], BF16, st)
        bf["mean3"] = sb("mean3", [128, 4], F32, st)
        bf["var3"] = sb("var3", [128, 4], F32, st)
        bf["rstd3"] = sb("rstd3", [128, 4], F32, st)
        bf["t1_sb"] = sb("t1_sb", [128, DC * 512], BF16, st)
        bf["t2_sb"] = sb("t2_sb", [128, DC * 512], BF16, st)
        bf["h_sb"] = sb("h_sb", [128, IC * 512], BF16, st)
        bf["sil_sb"] = sb("sil_sb", [128, 512], BF16, st)
        bf["wst"] = sb("wst", [128, 4 * DC * 128], BF16, st)
        bf["wstd"] = sb("wstd", [128, 2 * IC * 128], BF16, st)
        bf["ffn_ev"] = sb("ffn_ev", [128, 2 * 512], BF16, st)
        bf["ps3_a"] = psum("ps3_a", [128, 512], F32, st)
        bf["ps3_b"] = psum("ps3_b", [128, 512], F32, st)
        bf["ps3_c"] = psum("ps3_c", [128, 512], F32, st)
        bf["ps_tr0"] = psum("ps3_tr0", [128, 1024], BF16, st)
        bf["ps_tr1"] = psum("ps3_tr1", [128, 1024], BF16, st)

    def alloc_p4(st):
        bf["av_sb"] = sb("av_sb", [128, DC * 2048], BF16, st)
        bf["avtmp"] = sb("avtmp", [128, 2 * 2048], BF16, st)
        bf["owT_sb"] = sb("owT_sb", [128, DC * 2048], BF16, st)
        bf["ff4_sb"] = sb("ff4_sb", [128, 2 * D], BF16, st)
        bf["x4_sb"] = sb("x4_sb", [128, 2 * D], F32, st)
        bf["fin_sb"] = sb("fin_sb", [128, 2 * D], F32, st)
        bf["ps4_0"] = psum("ps4_0", [128, 512], F32, st)
        bf["ps4_1"] = psum("ps4_1", [128, 512], F32, st)

    def run_pass():
        with ExitStack() as st:
            alloc_p0(st)
            if em.emit:
                with nc.Block() as blk:
                    blk.gpsimd(lambda g: p0_gp(g))
                with nc.Block() as blk:
                    blk.tensor(lambda t: p0c_pe(t))
                    blk.vector(lambda v: p0c_dve(v))
                    blk.scalar(lambda a: p0c_act(a))
            else:
                p0_gp(None)
                p0c_pe(None)
                p0c_dve(None)
                p0c_act(None)
        if part in (0, 1):
            with ExitStack() as st:
                alloc_p1(st)
                if em.emit:
                    with nc.Block() as blk:
                        blk.scalar(lambda a: (p1_scalar_dma(a), p1_act(a)))
                        blk.sync(lambda s: p1_sync(s))
                        blk.vector(lambda v: p1_dve(v))
                        blk.tensor(lambda t: p1_tensor(t))
                else:
                    p1_scalar_dma(None)
                    p1_act(None)
                    p1_sync(None)
                    p1_dve(None)
                    p1_tensor(None)
            with ExitStack() as st:
                alloc_p2(st)
                if em.emit:
                    with nc.Block() as blk:
                        blk.sync(lambda s: p2_sync(s))
                        blk.tensor(lambda t: p2_tensor(t))
                        blk.scalar(lambda a: p2_act(a))
                        blk.gpsimd(lambda g: p2_gp(g))
                        blk.vector(lambda v: p2_dve(v))
                else:
                    p2_sync(None)
                    p2_tensor(None)
                    p2_act(None)
                    p2_gp(None)
                    p2_dve(None)
        if part == 2:
            with ExitStack() as st:
                bf["wprep"] = sb("wprep", [128, 2 * INTER], BF16, st)
                if em.emit:
                    with nc.Block() as blk:
                        blk.sync(lambda s: prep_standalone(s))
                else:
                    prep_standalone(None)
        if part in (0, 2):
            with ExitStack() as st:
                alloc_p3(st)
                if em.emit:
                    with nc.Block() as blk:
                        if part == 0:
                            blk.gpsimd(lambda g: p3_gp(g))
                        blk.sync(lambda s: p3_sync(s))
                        blk.scalar(lambda a: p3_scalar(a))
                        blk.vector(lambda v: p3_dve(v))
                        blk.tensor(lambda t: p3_tensor(t))
                else:
                    if part == 0:
                        p3_gp(None)
                    p3_sync(None)
                    p3_scalar(None)
                    p3_dve(None)
                    p3_tensor(None)
            with ExitStack() as st:
                alloc_p4(st)
                if em.emit:
                    with nc.Block() as blk:
                        blk.sync(lambda s: p4_sync(s))
                        blk.scalar(lambda a: p4_scalar(a))
                        blk.tensor(lambda t: p4_tensor(t))
                        blk.vector(lambda v: (p4_dve_combine(v), p4_dve(v)))
                else:
                    p4_sync(None)
                    p4_scalar(None)
                    p4_tensor(None)
                    p4_dve_combine(None)
                    p4_dve(None)
        if part == 1:
            pass

    em.emit = False
    run_pass()
    em.reset_counts()
    em.emit = True
    run_pass()

    es.close()
    return nc


_PROG_CACHE = {}


def kernel(**inputs):
    xs = np.ascontiguousarray(np.asarray(inputs["L0_attn_hidden_states"], np.float32))
    qkv_w = np.asarray(inputs["L0_attn_qkv_w"], np.float32)
    qkv_b = np.asarray(inputs["L0_attn_qkv_b"], np.float32)
    out_w = np.ascontiguousarray(np.asarray(inputs["L0_attn_out_w"], np.float32))
    out_b = np.ascontiguousarray(np.asarray(inputs["L0_attn_out_b"], np.float32))
    slopes = np.asarray(inputs["L0_attn_alibi_slopes"], np.float32)
    su_ug = np.ascontiguousarray(np.asarray(inputs["L0_ffn_SU_ug"], np.float32))
    su_dp = np.ascontiguousarray(np.asarray(inputs["L0_ffn_SU_dp"], np.float32))
    ws_ug = np.ascontiguousarray(np.asarray(inputs["L0_ffn_Wscale_ug"], np.float32))
    ws_dp = np.ascontiguousarray(np.asarray(inputs["L0_ffn_Wscale_dp"], np.float32))
    w_up = np.ascontiguousarray(np.asarray(inputs["L0_ffn_W_up"], np.float32))
    w_gate = np.ascontiguousarray(np.asarray(inputs["L0_ffn_W_gate"], np.float32))
    w_down = np.ascontiguousarray(np.asarray(inputs["L0_ffn_W_down"], np.float32))

    GJ, WTJ, WQJ = _head_params(slopes)
    fused = os.environ.get("K_FUSED", "0") == "1"
    if fused:
        key = (tuple(GJ), tuple(WTJ), tuple(WQJ), 0)
        if key not in _PROG_CACHE:
            _PROG_CACHE[key] = build_program(GJ, WTJ, WQJ, 0)
        nc = _PROG_CACHE[key]
    else:
        key1 = (tuple(GJ), tuple(WTJ), tuple(WQJ), 1)
        key2 = (tuple(GJ), tuple(WTJ), tuple(WQJ), 2)
        if key1 not in _PROG_CACHE:
            _PROG_CACHE[key1] = build_program(GJ, WTJ, WQJ, 1)
        if key2 not in _PROG_CACHE:
            _PROG_CACHE[key2] = build_program(GJ, WTJ, WQJ, 2)
        nc1, nc2 = _PROG_CACHE[key1], _PROG_CACHE[key2]

    h128 = _hadamard(128)
    iden = np.eye(128, dtype=np.float32)

    in_maps = []
    for c in range(8):
        b, qi = c // 4, c % 4
        heads = [4 * j + qi for j in range(NHL)]
        rows_q = np.concatenate([qkv_w[128 * h0:128 * (h0 + 1)] for h0 in heads])
        rows_k = np.concatenate([qkv_w[D + 128 * h0:D + 128 * (h0 + 1)] for h0 in heads])
        rows_v = np.concatenate([qkv_w[2 * D + 128 * h0:2 * D + 128 * (h0 + 1)]
                                 for h0 in heads])
        bq = np.concatenate([qkv_b[128 * h0:128 * (h0 + 1)] for h0 in heads])
        bk = np.concatenate([qkv_b[D + 128 * h0:D + 128 * (h0 + 1)] for h0 in heads])
        bv = np.concatenate([qkv_b[2 * D + 128 * h0:2 * D + 128 * (h0 + 1)]
                             for h0 in heads])
        in_maps.append({
            "x_batch": np.ascontiguousarray(xs[b].reshape(S, D)),
            "x_own": np.ascontiguousarray(xs[b, 2048 * qi:2048 * (qi + 1)].reshape(OWN, D)),
            "wq_my": np.ascontiguousarray(rows_q),
            "wk_my": np.ascontiguousarray(rows_k),
            "wv_my": np.ascontiguousarray(rows_v),
            "bq_my": np.ascontiguousarray(bq),
            "bk_my": np.ascontiguousarray(bk),
            "bv_my": np.ascontiguousarray(bv),
            "slopes_my": np.ascontiguousarray(slopes[heads]),
            "out_w": out_w,
            "out_b": out_b,
            "w_up": w_up,
            "w_gate": w_gate,
            "w_down": w_down,
            "su_ug": su_ug,
            "su_dp": su_dp,
            "ws_ug": ws_ug,
            "ws_dp": ws_dp,
            "h128": h128,
            "iden": iden,
            "bm0": np.full((128, 1), 1.0 if b == 0 else 0.0, np.float32),
            "bm1": np.full((128, 1), 1.0 if b == 1 else 0.0, np.float32),
        })

    if fused:
        res = run_bass_kernel_spmd(nc, in_maps, core_ids=list(range(8)))
        out = np.empty((B, S, D), dtype=np.float32)
        for c in range(8):
            b, qi = c // 4, c % 4
            out[b, 2048 * qi:2048 * (qi + 1)] = res.results[c]["y_own"]
        return out

    res1 = run_bass_kernel_spmd(nc1, in_maps, core_ids=list(range(8)))
    avl = [np.asarray(res1.results[c]["av_local"]) for c in range(8)]
    for c in range(8):
        b, qi = c // 4, c % 4
        av_in = np.empty((OWN, OWN), dtype=avl[0].dtype)
        for qi_s in range(4):
            av_in[512 * qi_s:512 * (qi_s + 1), :] = \
                avl[4 * b + qi_s][512 * qi:512 * (qi + 1), :]
        in_maps[c]["av_in"] = av_in
    res2 = run_bass_kernel_spmd(nc2, in_maps, core_ids=list(range(8)))
    out = np.empty((B, S, D), dtype=np.float32)
    for c in range(8):
        b, qi = c // 4, c % 4
        out[b, 2048 * qi:2048 * (qi + 1)] = res2.results[c]["y_own"]
    return out
